# version 08
# baseline (speedup 1.0000x reference)
import contextlib
import numpy as np
import concourse.bass as bass
import concourse.mybir as mybir
from concourse.bass_utils import run_bass_kernel_spmd

F32 = mybir.dt.float32
BF16 = mybir.dt.bfloat16
AF = mybir.ActivationFunctionType
AL = mybir.AluOpType
AX = mybir.AxisListType

SEQ = 2048
NSEQ = 2
D = 1024
KC = 8
NT = SEQ // 128
FT = SEQ // 512
EPS = 1e-5
NEWTON = 4

AB = 8 * (SEQ + 1)
P_XNJ = AB
P_KACT = AB + 1024
P_TANH = AB + 5120
P_XAB = AB + 7168
P_XAB2 = AB + 11264
NBF = AB + 15360

NF32 = 4096


def build_nc(debug=False, stop=None):
    nc = bass.Bass("TRN2", target_bir_lowering=False, detect_race_conditions=False)

    x_h = nc.dram_tensor("x", [NSEQ * SEQ, D], F32, kind="ExternalInput")
    w_h = {}
    for nm in ("wq", "wk", "wv", "wr", "wkf", "wvf"):
        w_h[nm] = nc.dram_tensor(nm, [D, D], F32, kind="ExternalInput")
    mur_h = nc.dram_tensor("mur", [D], F32, kind="ExternalInput")
    muk_h = nc.dram_tensor("muk", [D], F32, kind="ExternalInput")
    gam_h = nc.dram_tensor("gamma", [1], F32, kind="ExternalInput")
    idf_h = nc.dram_tensor("ident", [128, 128], F32, kind="ExternalInput")
    out_h = nc.dram_tensor("out", [NSEQ * SEQ, D], F32, kind="ExternalOutput")
    wT_h = {nm: nc.dram_tensor(nm + "T", [D, D], BF16, kind="Internal")
            for nm in ("wq", "wk", "wv", "wr", "wkf", "wvf")}

    stack = contextlib.ExitStack()
    sb = lambda *a, **k: stack.enter_context(nc.sbuf_tensor(*a, **k))
    psal = lambda *a, **k: stack.enter_context(nc.psum_tensor(*a, **k))

    fT = sb("fT", [128, KC, SEQ], BF16)
    gT = sb("gT", [128, KC, SEQ], BF16)
    hTok = sb("hTok", [128, NT, D], BF16)
    ffnO = sb("ffnO", [128, NT, D], BF16)
    identb = sb("identb", [128, 128], BF16)
    mus = sb("mus", [128, 32], F32)
    gam128 = sb("gam128", [128, 1], F32)
    s1a = sb("s1a", [128, 2 * NT], F32)
    aa = sb("aa", [128, 2 * NT], F32)
    rstd = sb("rstd", [128, NT], F32)
    sstat = sb("sstat", [128, 2 * NT], F32)
    lns = sb("lns", [128, 96], F32)
    negmax = sb("negmax", [128, 2], F32)
    sumexp = sb("sumexp", [128, 2], F32)
    sinv = sb("sinv", [128, 1], F32)
    factor = sb("factor", [128, 1], F32)
    pbf = sb("pbf", [128, NBF], BF16)
    pf = sb("pf", [128, NF32], F32)

    ps_sc = psal("ps_sc", [128, 2048], F32)
    ps_am = psal("ps_am", [128, 1024], F32)
    ps_tr = [psal("ps_tr0", [128, 1024], BF16),
             psal("ps_tr1", [128, 1024], BF16)]

    xnT3 = pbf[:, 0:AB].rearrange("q (k t) -> q k t", t=SEQ + 1)

    def xn_cur(k, a, b):
        return pbf[:, k * (SEQ + 1) + 1 + a: k * (SEQ + 1) + 1 + b]

    def xn_prv(k, a, b):
        return pbf[:, k * (SEQ + 1) + a: k * (SEQ + 1) + b]

    xnj = lambda p: pbf[:, P_XNJ: P_XNJ + D]
    kact3 = pbf[:, P_KACT:P_KACT + 4096].rearrange("q (k t) -> q k t", t=512)
    whalf3 = pbf[:, P_KACT:P_KACT + 4096].rearrange("q (k d) -> q k d", d=512)
    tanh4 = pbf[:, P_TANH:P_TANH + 2048].rearrange("q (a d) -> q a d", d=512)
    sqjunk = pbf[:, P_TANH:P_TANH + 1024]
    krelu = pbf[:, P_TANH:P_TANH + 512]
    xabA = pbf[:, P_XAB:P_XAB + 4096].rearrange("q (k t) -> q k t", t=512)
    xabB = pbf[:, P_XAB2:P_XAB2 + 4096].rearrange("q (k t) -> q k t", t=512)
    cblk = lambda p: pbf[:, P_XAB + p * 1024: P_XAB + (p + 1) * 1024].rearrange(
        "q (k d) -> q k d", d=128)
    Eb = pbf[:, 0:2048]
    eT = pbf[:, 2048:4096]
    hT_flat = hTok.rearrange("q a b -> q (a b)")
    wc = lambda p: hT_flat[:, p * 4096:(p + 1) * 4096]
    wt = lambda p: hT_flat[:, 8192 + p * 4096: 8192 + (p + 1) * 4096]
    fT_flat = fT.rearrange("q a b -> q (a b)")
    gT_flat = gT.rearrange("q a b -> q (a b)")
    wkf_w = fT_flat[:, 0:8192].rearrange("q (k d) -> q k d", d=D)
    wr_w = fT_flat[:, 8192:16384].rearrange("q (k d) -> q k d", d=D)
    wvf_w = gT_flat[:, 0:8192].rearrange("q (k d) -> q k d", d=D)

    x2b = pf[:, 0:2048].rearrange("q (a c) -> q a c", c=D)
    x_in = lambda p: pf[:, p * D:(p + 1) * D]
    outst = lambda p: pf[:, 2048 + p * D: 2048 + (p + 1) * D]
    identf = pf[:, 2048:2176]
    mur8 = pf[0:8, 2176:2304]
    muk8 = pf[0:8, 2304:2432]
    ones1 = pf[0:1, 2432:2560]
    g1 = pf[0:1, 2560:2561]

    sem_names = ("d_init", "d_wc0", "d_wc1", "d_rpkf", "d_rpr", "d_rpvf",
                 "d_rp2", "d_rp3", "d_x2b", "d_cb0", "d_cb1", "d_wh", "d_fw0",
                 "d_fw1", "d_fw2", "d_xi0", "d_xi1", "d_st0", "d_st1",
                 "t", "v", "s", "g")
    SEM = {nm: stack.enter_context(nc.semaphore(nm)) for nm in sem_names}

    ops = {"sync": [], "gpsimd": [], "scalar": [], "vector": [], "tensor": []}
    cnt = {nm: 0 for nm in sem_names}

    def rec(eng, fn, inc=None, by=1):
        if inc is None:
            ops[eng].append(fn)
            return None
        cnt[inc] += by
        val = cnt[inc]
        sem = SEM[inc]

        def run(e, fn=fn, sem=sem, by=by):
            fn(e).then_inc(sem, by)
        ops[eng].append(run)
        return val

    def wait(eng, sem, val):
        if val is not None and val > 0:
            ops[eng].append(lambda e, s=SEM[sem], v=val: e.wait_ge(s, v))

    def fence_all():
        for e in ops:
            for snm in sem_names:
                wait(e, snm, cnt[snm])

    def dma(queue_eng, qsem, out_ap, in_ap):
        return rec(queue_eng, lambda e, o=out_ap, i=in_ap: e.dma_start(o, i),
                   inc=qsem, by=16)

    def vchain(fn):
        vv = rec("vector", fn, inc="v")
        wait("vector", "v", vv)
        return vv

    rfence = {}

    def rwait(eng, name):
        f = rfence.get(name)
        if f is not None:
            wait(eng, f[0], f[1])

    def rset(name, sem, val):
        rfence[name] = (sem, val)

    dma("sync", "d_init", identf, idf_h[:, :])
    dma("sync", "d_init", mur8, mur_h.ap().rearrange("(k p) -> k p", p=128))
    dma("sync", "d_init", muk8, muk_h.ap().rearrange("(k p) -> k p", p=128))
    dma("sync", "d_init", g1, gam_h[0:1])

    rec("gpsimd", lambda e: e.memset(ones1, 1.0), inc="g")

    wait("scalar", "d_init", 64)
    v_idb = rec("scalar", lambda e: e.copy(identb[:, :], identf), inc="s")

    wait("tensor", "d_init", 64)
    rec("tensor", lambda e: e.transpose(ps_am[:, 0:8], mur8, identf[0:8, 0:8]))
    rec("tensor", lambda e: e.transpose(ps_am[:, 8:16], muk8, identf[0:8, 0:8]))
    wait("tensor", "g", 1)
    t_mu = rec("tensor", lambda e: e.matmul(ps_am[:, 16:17], ones1, g1,
                                            start=True, stop=True), inc="t")
    wait("vector", "t", t_mu)
    vchain(lambda e: e.tensor_copy(mus[:, 0:8], ps_am[:, 0:8]))
    vchain(lambda e: e.tensor_copy(mus[:, 8:16], ps_am[:, 8:16]))
    vchain(lambda e: e.tensor_copy(gam128[:, 0:1], ps_am[:, 16:17]))
    vchain(lambda e: e.tensor_scalar(mus[:, 16:24], mus[:, 0:8], -1.0, 1.0, AL.mult, AL.add))
    v_mu = vchain(lambda e: e.tensor_scalar(mus[:, 24:32], mus[:, 8:16], -1.0, 1.0,
                                            AL.mult, AL.add))
    rset("ps_am", "v", v_mu)

    def emit_pass1_group(sq, grp):
        row0 = sq * SEQ
        rwait("sync", "x2b_v")
        rwait("sync", "x2b_s")
        qv = dma("sync", "d_x2b", x2b,
                 x_h[row0 + grp * 256: row0 + (grp + 1) * 256, :]
                 .rearrange("(a p) c -> p a c", p=128))
        wait("vector", "d_x2b", qv)
        rec("vector", lambda e, grp=grp: e.reduce_sum(
            sstat[:, 2 * grp:2 * grp + 1], x2b[:, 0, :], axis=AX.X))
        v_red = rec("vector", lambda e, grp=grp: e.reduce_sum(
            sstat[:, 2 * grp + 1:2 * grp + 2], x2b[:, 1, :], axis=AX.X), inc="v")
        wait("scalar", "d_x2b", qv)
        rwait("scalar", "sqjunk")
        rec("scalar", lambda e, grp=grp: e.activation(
            sqjunk, x2b[:, 0, :], AF.Square, accum_out=sstat[:, NT + 2 * grp:NT + 2 * grp + 1]))
        s_sq = rec("scalar", lambda e, grp=grp: e.activation(
            sqjunk, x2b[:, 1, :], AF.Square,
            accum_out=sstat[:, NT + 2 * grp + 1:NT + 2 * grp + 2]), inc="s")
        rset("x2b_v", "v", v_red)
        rset("x2b_s", "s", s_sq)
        return v_red, s_sq

    def emit_stats(sq, v_red, s_sq):
        base = sq * NT
        wait("vector", "s", s_sq)
        wait("vector", "v", v_red)
        mean_ = lns[:, 0:16]
        ex2_ = lns[:, 16:32]
        msq_ = lns[:, 32:48]
        veps_ = lns[:, 48:64]
        t1_ = lns[:, 64:80]
        t2_ = lns[:, 80:96]
        y_ = rstd[:, :]
        vchain(lambda e: e.tensor_scalar_mul(lns[:, 0:32], sstat[:, :], 1.0 / D))
        vchain(lambda e: e.tensor_mul(msq_, mean_, mean_))
        vchain(lambda e: e.tensor_sub(veps_, ex2_, msq_))
        vchain(lambda e: e.tensor_scalar_add(veps_, veps_, EPS))
        vchain(lambda e: e.tensor_scalar(t1_, veps_, 0.5, 0.5, AL.mult, AL.add))
        vchain(lambda e: e.reciprocal(y_, t1_))
        for _ in range(NEWTON):
            vchain(lambda e: e.tensor_mul(t1_, y_, y_))
            vchain(lambda e: e.tensor_mul(t2_, t1_, veps_))
            vchain(lambda e: e.tensor_scalar(t1_, t2_, -0.5, 1.5, AL.mult, AL.add))
            vchain(lambda e: e.tensor_mul(y_, y_, t1_))
        vchain(lambda e: e.tensor_scalar_add(s1a[:, base:base + NT], y_, 1.0))
        return vchain(lambda e: e.tensor_mul(aa[:, base:base + NT], mean_, y_))

    v_xnT = {}

    def emit_pass2_group(sq, grp):
        row0 = sq * SEQ
        rwait("sync", "x2b_v")
        qv = dma("sync", "d_x2b", x2b,
                 x_h[row0 + grp * 256: row0 + (grp + 1) * 256, :]
                 .rearrange("(a p) c -> p a c", p=128))
        wait("vector", "d_x2b", qv)
        base = sq * NT
        v_last = None
        for i in (0, 1):
            ti = 2 * grp + i
            rwait("vector", "xnj0")
            v_norm = rec("vector", lambda e, i=i, ti=ti, base=base: e.tensor_scalar(
                xnj(0), x2b[:, i, :],
                rstd[:, ti:ti + 1], aa[:, base + ti:base + ti + 1],
                AL.mult, AL.subtract), inc="v")
            wait("tensor", "v", v_norm)
            wait("tensor", "s", v_idb)
            rwait("tensor", "ps_tr%d" % (ti % 2))
            t_xnt = None
            for kk in range(KC):
                t_xnt = rec("tensor", lambda e, ti=ti, kk=kk: e.transpose(
                    ps_tr[ti % 2][:, kk * 128:(kk + 1) * 128],
                    xnj(0)[:, kk * 128:(kk + 1) * 128], identb[:, :]),
                    inc="t" if kk == KC - 1 else None)
            rset("xnj0", "t", t_xnt)
            wait("vector", "t", t_xnt)
            v_xnT[ti] = rec("vector", lambda e, ti=ti: e.tensor_copy(
                xnT3[:, :, 1 + ti * 128: 1 + (ti + 1) * 128],
                ps_tr[ti % 2][:, :].rearrange("q (k t) -> q k t", t=128)), inc="v")
            rset("ps_tr%d" % (ti % 2), "v", v_xnT[ti])
            v_last = v_xnT[ti]
        rset("x2b_v", "v", v_last)

    def emit_shift_build(tt, mu_off, xab):
        for k in range(KC):
            rec("vector", lambda e, k=k, tt=tt, xab=xab: e.tensor_sub(
                xab[:, k, :],
                xn_cur(k, tt * 512, (tt + 1) * 512),
                xn_prv(k, tt * 512, (tt + 1) * 512)))
        v_x = None
        for k in range(KC):
            v_x = rec("vector", lambda e, k=k, tt=tt, mu_off=mu_off, xab=xab:
                      e.scalar_tensor_tensor(
                xab[:, k, :], xab[:, k, :], mus[:, mu_off + k:mu_off + k + 1],
                xn_prv(k, tt * 512, (tt + 1) * 512), AL.mult, AL.add), inc="v")
        return v_x

    last_v_ka = [None]

    def emit_K(tt, v_xk, qfw):
        t_K_last = None
        for j in range(KC):
            b = j % 4
            wait("tensor", "d_fw0", qfw[0])
            wait("tensor", "v", v_xk)
            rwait("tensor", "bank%d" % b)
            t_last = None
            for k in range(KC):
                t_last = rec("tensor", lambda e, j=j, k=k, b=b: e.matmul(
                    ps_sc[:, b * 512:(b + 1) * 512],
                    wkf_w[:, k, j * 128:(j + 1) * 128], xabA[:, k, :],
                    start=(k == 0), stop=(k == KC - 1)),
                    inc="t" if k == KC - 1 else None)
            t_K_last = t_last
            wait("scalar", "t", t_last)
            if last_v_ka[0] is not None:
                wait("scalar", "v", last_v_ka[0])
            s_kr = rec("scalar", lambda e, b=b: e.activation(
                krelu, ps_sc[:, b * 512:(b + 1) * 512], AF.Relu), inc="s")
            rset("bank%d" % b, "s", s_kr)
            wait("vector", "s", s_kr)
            last_v_ka[0] = rec("vector", lambda e, j=j: e.tensor_mul(
                kact3[:, j, :], krelu, krelu), inc="v")
        rset("xabA", "t", t_K_last)
        return t_K_last

    def emit_R(tt, h, v_xr, qfw):
        t_R_last = None
        s_tanh = {}
        for tq in range(4):
            ab = tq % 2
            wait("tensor", "d_fw1", qfw[1])
            wait("tensor", "v", v_xr)
            rwait("tensor", "amb%d" % ab)
            t_last = None
            for k in range(KC):
                t_last = rec("tensor", lambda e, k=k, tq=tq, ab=ab, h=h: e.matmul(
                    ps_am[:, ab * 512:(ab + 1) * 512],
                    xabB[:, k, tq * 128:(tq + 1) * 128],
                    wr_w[:, k, h * 512:(h + 1) * 512],
                    start=(k == 0), stop=(k == KC - 1)),
                    inc="t" if k == KC - 1 else None)
            t_R_last = t_last
            wait("scalar", "t", t_last)
            rwait("scalar", "tanh%d" % tq)
            s_tanh[tq] = rec("scalar", lambda e, tq=tq, ab=ab: e.activation(
                tanh4[:, tq, :], ps_am[:, ab * 512:(ab + 1) * 512],
                AF.Tanh, scale=0.5), inc="s")
            rset("amb%d" % ab, "s", s_tanh[tq])
        return t_R_last, s_tanh

    def emit_V2(tt, h, s_tanh, qfw):
        t_V_last = None
        for tq in range(4):
            b = tq % 4
            wait("tensor", "d_fw2", qfw[2])
            rwait("tensor", "bank%d" % b)
            t_last = None
            for k in range(KC):
                t_last = rec("tensor", lambda e, k=k, tq=tq, b=b, h=h: e.matmul(
                    ps_sc[:, b * 512:(b + 1) * 512],
                    kact3[:, k, tq * 128:(tq + 1) * 128],
                    wvf_w[:, k, h * 512:(h + 1) * 512],
                    start=(k == 0), stop=(k == KC - 1)),
                    inc="t" if k == KC - 1 else None)
            t_V_last = t_last
            wait("vector", "t", t_last)
            wait("vector", "s", s_tanh[tq])
            vv = rec("vector", lambda e, tt=tt, tq=tq, h=h, b=b: e.scalar_tensor_tensor(
                ffnO[:, tt * 4 + tq, h * 512:(h + 1) * 512],
                tanh4[:, tq, :], 1.0, ps_sc[:, b * 512:(b + 1) * 512],
                AL.add, AL.mult), inc="v")
            rset("bank%d" % b, "v", vv)
            rset("tanh%d" % tq, "v", vv)
            rset("sqjunk", "v", vv)
        return t_V_last

    repack_list = [("wkf", False), ("wr", False), ("wvf", True),
                   ("wq", False), ("wk", False), ("wv", False)]
    rp_iter = [(nm, sc, j) for nm, sc in repack_list for j in range(KC)]

    def emit_repack(i):
        nm, scale_half, j = rp_iter[i]
        par = i % 2
        rp_sem = (("d_rpkf", "d_rpr", "d_rpvf")[i // 8] if i < 24
                  else "d_rp%d" % (par + 2))
        rwait("gpsimd", "wcast%d" % par)
        qv = dma("gpsimd", "d_wc%d" % par, wc(par)[:, 0:1024],
                 w_h[nm][j * 128:(j + 1) * 128, :])
        wait("tensor", "d_wc%d" % par, qv)
        wait("tensor", "s", v_idb)
        rwait("tensor", "ps_tr%d" % par)
        tv = None
        for kk in range(KC):
            tv = rec("tensor",
                     lambda e, par=par, kk=kk: e.transpose(
                         ps_tr[par][:, kk * 128:(kk + 1) * 128],
                         wc(par)[:, kk * 128:(kk + 1) * 128], identb[:, :]),
                     inc="t" if kk == KC - 1 else None)
        rset("wcast%d" % par, "t", tv)
        wait("scalar", "t", tv)
        rwait("scalar", "wtmp%d" % par)
        if scale_half:
            sv2 = rec("scalar", lambda e, par=par: e.mul(
                wt(par)[:, 0:1024], ps_tr[par][:, :], 0.5), inc="s")
        else:
            sv2 = rec("scalar", lambda e, par=par: e.copy(
                wt(par)[:, 0:1024], ps_tr[par][:, :]), inc="s")
        rset("ps_tr%d" % par, "s", sv2)
        wait("sync", "s", sv2)
        sv = dma("sync", rp_sem,
                 wT_h[nm][:, j * 128:(j + 1) * 128].rearrange("(k p) d -> p k d", p=128),
                 wt(par)[:, 0:1024].rearrange("q (k d) -> q k d", d=128))
        rset("wtmp%d" % par, rp_sem, cnt[rp_sem])

    v_red = s_sq = None
    for grp in range(8):
        v_red, s_sq = emit_pass1_group(0, grp)
        for i in range(6 * grp, 6 * grp + 6):
            emit_repack(i)
    emit_stats(0, v_red, s_sq)

    for sq in range(NSEQ):
        row0 = sq * SEQ
        if sq > 0:
            fence_all()
            rfence.clear()

        for k in range(KC):
            rec("gpsimd", lambda e, k=k: e.memset(xnT3[:, k, 0:1], 0.0),
                inc="g" if k == KC - 1 else None)
        g_z = cnt["g"]

        last_v_ka[0] = None
        emit_pass2_group(sq, 0)
        emit_pass2_group(sq, 1)
        rwait("sync", "fTw")
        rwait("sync", "gTw")
        wait("sync", "d_rpkf", 16 * 8)
        q0 = dma("sync", "d_fw0", wkf_w,
                 wT_h["wkf"].ap().rearrange("(k p) d -> p k d", p=128))
        wait("sync", "d_rpr", 16 * 8)
        q1 = dma("sync", "d_fw1", wr_w,
                 wT_h["wr"].ap().rearrange("(k p) d -> p k d", p=128))
        wait("sync", "d_rpvf", 16 * 8)
        q2 = dma("sync", "d_fw2", wvf_w,
                 wT_h["wvf"].ap().rearrange("(k p) d -> p k d", p=128))
        qfw = (q0, q1, q2)
        wait("vector", "g", g_z)
        rwait("vector", "xabA")
        wait("vector", "v", v_xnT[3])
        v_xk = emit_shift_build(0, 8, xabA)
        rwait("vector", "xabB")
        v_xr = emit_shift_build(0, 0, xabB)
        t_K_f = t_R_f = t_V_f = None
        for tt in range(FT):
            t_K_f = emit_K(tt, v_xk, qfw)
            if 2 * tt + 2 < 8:
                emit_pass2_group(sq, 2 * tt + 2)
                emit_pass2_group(sq, 2 * tt + 3)
            t_R0, s_tanh0 = emit_R(tt, 0, v_xr, qfw)
            if tt + 1 < FT:
                rwait("vector", "xabA")
                wait("vector", "g", g_z)
                wait("vector", "v", v_xnT[4 * tt + 7])
                v_xk = emit_shift_build(tt + 1, 8, xabA)
            t_V0 = emit_V2(tt, 0, s_tanh0, qfw)
            t_R_f, s_tanh1 = emit_R(tt, 1, v_xr, qfw)
            rset("xabB", "t", t_R_f)
            if tt + 1 < FT:
                rwait("vector", "xabB")
                v_xr = emit_shift_build(tt + 1, 0, xabB)
            t_V_f = emit_V2(tt, 1, s_tanh1, qfw)
            rset("kact", "t", t_V_f)
        if stop == "pa3" and sq == 0:
            break

        fence_fT = ("t", t_R_f)
        fence_gT = ("t", t_V_f)
        mgrp = [0]
        p1_grp = [0]
        v_red1 = [None]
        s_sq1 = [None]
        for nm, dest, wfence in (("wq", fT, fence_fT), ("wk", gT, fence_gT)):
            for j in range(KC):
                par = j % 2
                if sq == 0 and j % 2 == 0 and p1_grp[0] < 8:
                    v_red1[0], s_sq1[0] = emit_pass1_group(1, p1_grp[0])
                    p1_grp[0] += 1
                rwait("sync", "colblk%d" % par)
                if j < 2:
                    wait("sync", "t", t_K_f)
                    wait("sync", "d_rp2", 16 * 12)
                    wait("sync", "d_rp3", 16 * 12)
                qv = dma("sync", "d_cb%d" % par, cblk(par),
                         wT_h[nm][:, j * 128:(j + 1) * 128].rearrange("(k p) d -> p k d", p=128))
                t_last = None
                for tt in range(4):
                    b = mgrp[0] % 4
                    mgrp[0] += 1
                    wait("tensor", "d_cb%d" % par, qv)
                    wait("tensor", "v", v_xnT[4 * tt + 3])
                    rwait("tensor", "bank%d" % b)
                    for k in range(KC):
                        t_last = rec("tensor", lambda e, par=par, k=k, tt=tt, b=b: e.matmul(
                            ps_sc[:, b * 512:(b + 1) * 512],
                            cblk(par)[:, k, :],
                            xn_cur(k, tt * 512, (tt + 1) * 512),
                            start=(k == 0), stop=(k == KC - 1)),
                            inc="t" if k == KC - 1 else None)
                    wait("vector", "t", t_last)
                    if wfence is not None:
                        wait("vector", *wfence)
                        wfence = None
                    vv = rec("vector", lambda e, dest=dest, j=j, tt=tt, b=b: e.tensor_copy(
                        dest[:, j, tt * 512:(tt + 1) * 512],
                        ps_sc[:, b * 512:(b + 1) * 512]), inc="v")
                    rset("bank%d" % b, "v", vv)
                rset("colblk%d" % par, "t", t_last)

        for h in range(2):
            rwait("sync", "whalf")
            wait("sync", "t", t_V_f)
            qv = dma("sync", "d_wh", whalf3,
                     wT_h["wv"][:, h * 512:(h + 1) * 512].rearrange("(k p) d -> p k d", p=128))
            t_last = None
            for tq in range(NT):
                b = mgrp[0] % 4
                mgrp[0] += 1
                wait("tensor", "d_wh", qv)
                wait("tensor", "v", v_xnT[tq])
                rwait("tensor", "bank%d" % b)
                for k in range(KC):
                    t_last = rec("tensor", lambda e, k=k, tq=tq, b=b: e.matmul(
                        ps_sc[:, b * 512:(b + 1) * 512],
                        xn_cur(k, tq * 128, (tq + 1) * 128),
                        whalf3[:, k, :],
                        start=(k == 0), stop=(k == KC - 1)),
                        inc="t" if k == KC - 1 else None)
                wait("vector", "t", t_last)
                vv = rec("vector", lambda e, tq=tq, h=h, b=b: e.tensor_copy(
                    hTok[:, tq, h * 512:(h + 1) * 512],
                    ps_sc[:, b * 512:(b + 1) * 512]), inc="v")
                rset("bank%d" % b, "v", vv)
            rset("whalf", "t", t_last)
        if sq == 0:
            emit_stats(1, v_red1[0], s_sq1[0])

        fence_all()
        rfence.clear()
        qx = {0: dma("sync", "d_xi0", x_in(0), x_h[row0:row0 + 128, :]),
              1: dma("sync", "d_xi1", x_in(1), x_h[row0 + 128:row0 + 256, :])}
        t_S, t_ET, t_AM = {}, {}, {}
        v_mx, v_ev, v_cmb = {}, {}, {}
        s_exp, st_val = {}, {}

        def emit_AM(qq):
            wait("tensor", "v", v_ev[(qq, 1)])
            if qq >= 1:
                wait("tensor", "v", v_cmb[qq - 1])
            tv = None
            for h in range(2):
                for sc in range(NT):
                    tv = rec("tensor", lambda e, sc=sc, h=h: e.matmul(
                        ps_am[:, h * 512:(h + 1) * 512],
                        eT[:, sc * 128:(sc + 1) * 128],
                        hTok[:, sc, h * 512:(h + 1) * 512],
                        start=(sc == 0), stop=(sc == NT - 1)),
                        inc="t" if (h == 1 and sc == NT - 1) else None)
            t_AM[qq] = tv

        def emit_combine(qq):
            wait("vector", "t", t_AM[qq])
            wait("vector", "d_xi%d" % (qq % 2), qx[qq])
            wait("vector", "s", s_exp[qq])
            if qq >= 2:
                wait("vector", "d_st%d" % (qq % 2), st_val[qq - 2])
            vchain(lambda e, qq=qq: e.reciprocal(sinv[:, 0:1],
                                                 sumexp[:, qq % 2:qq % 2 + 1]))
            vchain(lambda e: e.tensor_scalar_mul(factor[:, 0:1], sinv[:, 0:1],
                                                 gam128[:, 0:1]))
            base = sq * NT
            rec("vector", lambda e, qq=qq, base=base: e.tensor_scalar(
                outst(qq % 2), x_in(qq % 2),
                s1a[:, base + qq:base + qq + 1], aa[:, base + qq:base + qq + 1],
                AL.mult, AL.subtract))
            rec("vector", lambda e, qq=qq: e.scalar_tensor_tensor(
                outst(qq % 2), ps_am[:, 0:1024], factor[:, 0:1],
                outst(qq % 2), AL.mult, AL.add))
            v_cmb[qq] = rec("vector", lambda e, qq=qq: e.tensor_add(
                outst(qq % 2), outst(qq % 2), ffnO[:, qq, :]), inc="v")

        def emit_store(qq):
            wait("scalar", "v", v_cmb[qq])
            st_val[qq] = dma("scalar", "d_st%d" % (qq % 2),
                             out_h[row0 + qq * 128: row0 + (qq + 1) * 128, :],
                             outst(qq % 2))

        for q in range(NT):
            if q >= 1:
                wait("tensor", "s", s_exp[q - 1])
            tv = None
            for j in range(4):
                for k in range(KC):
                    tv = rec("tensor", lambda e, j=j, k=k, q=q: e.matmul(
                        ps_sc[:, j * 512:(j + 1) * 512],
                        gT[:, k, q * 128:(q + 1) * 128],
                        fT[:, k, j * 512:(j + 1) * 512],
                        start=(k == 0), stop=(k == KC - 1)),
                        inc="t" if (j == 3 and k == KC - 1) else None)
            t_S[q] = tv
            wait("vector", "t", t_S[q])
            v_mx[q] = rec("vector", lambda e, q=q: e.reduce_max(
                negmax[:, q % 2:q % 2 + 1], ps_sc[:, 0:2048], axis=AX.X, negate=True),
                inc="v")
            wait("scalar", "v", v_mx[q])
            s_exp[q] = rec("scalar", lambda e, q=q: e.activation(
                Eb, ps_sc[:, 0:2048], AF.Exp,
                bias=negmax[:, q % 2:q % 2 + 1], scale=1.0,
                accum_out=sumexp[:, q % 2:q % 2 + 1]), inc="s")
            if q >= 1:
                emit_AM(q - 1)
            wait("tensor", "s", s_exp[q])
            for b in range(2):
                rwait("tensor", "ps_tr%d" % b)
                tv = None
                for kk in range(KC):
                    tv = rec("tensor", lambda e, b=b, kk=kk: e.transpose(
                        ps_tr[b][:, kk * 128:(kk + 1) * 128],
                        Eb[:, (b * KC + kk) * 128:(b * KC + kk + 1) * 128],
                        identb[:, :]), inc="t" if kk == KC - 1 else None)
                t_ET[(q, b)] = tv
            if q >= 1:
                emit_combine(q - 1)
            for b in range(2):
                wait("vector", "t", t_ET[(q, b)])
                v_ev[(q, b)] = rec("vector", lambda e, b=b: e.tensor_copy(
                    eT[:, b * 1024:(b + 1) * 1024], ps_tr[b][:, :]), inc="v")
                rset("ps_tr%d" % b, "v", v_ev[(q, b)])
            if q >= 1:
                emit_store(q - 1)
            if q + 1 < NT and q >= 1:
                wait("sync", "v", v_cmb[q - 1])
                qx[q + 1] = dma("sync", "d_xi%d" % ((q + 1) % 2), x_in((q + 1) % 2),
                                x_h[row0 + (q + 1) * 128: row0 + (q + 2) * 128, :])
        emit_AM(NT - 1)
        emit_combine(NT - 1)
        emit_store(NT - 1)
        rset("fTw", "t", t_S[NT - 1])
        rset("gTw", "t", t_S[NT - 1])
        if stop == "seq0" and sq == 0:
            break

    if debug:
        dbg_h = nc.dram_tensor("dbg", [128, 4096], F32, kind="ExternalOutput")
        dbgb_h = nc.dram_tensor("dbgb", [128, 16384], BF16, kind="ExternalOutput")
        fence_all()

        def dmad(o, i):
            return rec("sync", lambda e, o=o, i=i: e.dma_start(
                o, i, allow_slow_non_contiguous=True), inc="d_init", by=16)
        dmad(dbg_h[:, 0:2], negmax[:, :])
        dmad(dbg_h[:, 2:4], sumexp[:, :])
        dmad(dbg_h[:, 4:5], sinv[:, :])
        dmad(dbg_h[:, 5:6], factor[:, :])
        dmad(dbg_h[:, 6:7], gam128[:, :])
        dmad(dbg_h[:, 16:48], mus[:, :])
        dmad(dbg_h[:, 48:64], s1a[:, :])
        dmad(dbg_h[:, 64:80], aa[:, :])
        dmad(dbg_h[:, 80:176], lns[:, :])
        dmad(dbg_h[:, 176:192], rstd[:, :])
        dmad(dbg_h[:, 192:208], sums[:, :])
        dmad(dbg_h[:, 208:224], ssq[:, :])
        dmad(dbg_h[:, 1152:2176], outst(1))
        dmad(dbgb_h[:, 0:2048], Eb)
        dmad(dbgb_h[:, 2048:4096], eT)
        dmad(dbgb_h[:, 4096:5120], pbf[:, 0:1024])
        dmad(dbgb_h[:, 5120:6144], fT[:, 0, 0:1024])
        dmad(dbgb_h[:, 6144:7168], gT[:, 0, 0:1024])
        dmad(dbgb_h[:, 7168:7296], identb[:, :])
        dmad(dbgb_h[:, 8192:9216], hTok[:, 0, :])
        dmad(dbgb_h[:, 9216:10240], ffnO[:, 0, :])
        dmad(dbgb_h[:, 10240:14336], kact3[:, :, :])
        dmad(dbgb_h[:, 14336:15360], xnj(0))
        wait("sync", "d_init", cnt["d_init"])

    wait("scalar", "d_st0", cnt["d_st0"])
    wait("scalar", "d_st1", cnt["d_st1"])

    rem = nc.sbuf_bytes_remaining
    assert rem >= 0, f"SBUF over budget by {-rem} bytes"

    with nc.Block() as block:
        @block.sync
        def _(e):
            for f in ops["sync"]:
                f(e)

        @block.gpsimd
        def _(e):
            for f in ops["gpsimd"]:
                f(e)

        @block.scalar
        def _(e):
            for f in ops["scalar"]:
                f(e)

        @block.vector
        def _(e):
            for f in ops["vector"]:
                f(e)

        @block.tensor
        def _(e):
            for f in ops["tensor"]:
                f(e)

    return nc


def kernel(L0_attn_x, L0_attn_Wq, L0_attn_Wk, L0_attn_Wv, L0_attn_gamma,
           L0_ffn_W_r, L0_ffn_W_k, L0_ffn_W_v, L0_ffn_mu_r, L0_ffn_mu_k):
    x = np.ascontiguousarray(np.asarray(L0_attn_x, dtype=np.float32))
    ws = {
        "wq": np.ascontiguousarray(np.asarray(L0_attn_Wq, dtype=np.float32)),
        "wk": np.ascontiguousarray(np.asarray(L0_attn_Wk, dtype=np.float32)),
        "wv": np.ascontiguousarray(np.asarray(L0_attn_Wv, dtype=np.float32)),
        "wr": np.ascontiguousarray(np.asarray(L0_ffn_W_r, dtype=np.float32)),
        "wkf": np.ascontiguousarray(np.asarray(L0_ffn_W_k, dtype=np.float32)),
        "wvf": np.ascontiguousarray(np.asarray(L0_ffn_W_v, dtype=np.float32)),
    }
    mur = np.ascontiguousarray(np.asarray(L0_ffn_mu_r, dtype=np.float32))
    muk = np.ascontiguousarray(np.asarray(L0_ffn_mu_k, dtype=np.float32))
    gam = np.ascontiguousarray(np.asarray(L0_attn_gamma, dtype=np.float32))
    ident = np.eye(128, dtype=np.float32)

    nc = build_nc()
    in_maps = []
    for c in range(8):
        m = {"x": x[c * NSEQ:(c + 1) * NSEQ].reshape(NSEQ * SEQ, D),
             "mur": mur, "muk": muk, "gamma": gam, "ident": ident}
        m.update(ws)
        in_maps.append(m)
    res = run_bass_kernel_spmd(nc, in_maps, core_ids=list(range(8)))
    out = np.stack([np.asarray(r["out"]) for r in res.results])
    return out.reshape(16, SEQ, D)


# version 09
# speedup vs baseline: 1.0447x; 1.0447x over previous
import contextlib
import numpy as np
import concourse.bass as bass
import concourse.mybir as mybir
from concourse.bass_utils import run_bass_kernel_spmd

F32 = mybir.dt.float32
BF16 = mybir.dt.bfloat16
AF = mybir.ActivationFunctionType
AL = mybir.AluOpType
AX = mybir.AxisListType

SEQ = 2048
NSEQ = 2
D = 1024
KC = 8
NT = SEQ // 128
FT = SEQ // 512
EPS = 1e-5
NEWTON = 4

AB = 8 * (SEQ + 1)
P_XNJ = AB
P_KACT = AB + 1024
P_TANH = AB + 5120
P_XAB = AB + 7168
P_XAB2 = AB + 11264
NBF = AB + 15360

NF32 = 4096


def build_nc(debug=False, stop=None):
    nc = bass.Bass("TRN2", target_bir_lowering=False, detect_race_conditions=False)

    x_h = nc.dram_tensor("x", [NSEQ * SEQ, D], F32, kind="ExternalInput")
    w_h = {}
    for nm in ("wq", "wk", "wv", "wr", "wkf", "wvf"):
        w_h[nm] = nc.dram_tensor(nm, [D, D], F32, kind="ExternalInput")
    mur_h = nc.dram_tensor("mur", [D], F32, kind="ExternalInput")
    muk_h = nc.dram_tensor("muk", [D], F32, kind="ExternalInput")
    gam_h = nc.dram_tensor("gamma", [1], F32, kind="ExternalInput")
    idf_h = nc.dram_tensor("ident", [128, 128], F32, kind="ExternalInput")
    out_h = nc.dram_tensor("out", [NSEQ * SEQ, D], F32, kind="ExternalOutput")
    wT_h = {nm: nc.dram_tensor(nm + "T", [D, D], BF16, kind="Internal")
            for nm in ("wq", "wk", "wv", "wr", "wkf", "wvf")}

    stack = contextlib.ExitStack()
    sb = lambda *a, **k: stack.enter_context(nc.sbuf_tensor(*a, **k))
    psal = lambda *a, **k: stack.enter_context(nc.psum_tensor(*a, **k))

    fT = sb("fT", [128, KC, SEQ], BF16)
    gT = sb("gT", [128, KC, SEQ], BF16)
    hTok = sb("hTok", [128, NT, D], BF16)
    ffnO = sb("ffnO", [128, NT, D], BF16)
    identb = sb("identb", [128, 128], BF16)
    mus = sb("mus", [128, 32], F32)
    gam128 = sb("gam128", [128, 1], F32)
    s1a = sb("s1a", [128, 2 * NT], F32)
    aa = sb("aa", [128, 2 * NT], F32)
    rstd = sb("rstd", [128, NT], F32)
    sstat = sb("sstat", [128, 2 * NT], F32)
    lns = sb("lns", [128, 96], F32)
    negmax = sb("negmax", [128, 2], F32)
    sumexp = sb("sumexp", [128, 2], F32)
    sinv = sb("sinv", [128, 1], F32)
    factor = sb("factor", [128, 1], F32)
    pbf = sb("pbf", [128, NBF], BF16)
    pf = sb("pf", [128, NF32], F32)

    ps_sc = psal("ps_sc", [128, 2048], F32)
    ps_am = psal("ps_am", [128, 1024], F32)
    ps_tr = [psal("ps_tr0", [128, 1024], BF16),
             psal("ps_tr1", [128, 1024], BF16)]

    xnT3 = pbf[:, 0:AB].rearrange("q (k t) -> q k t", t=SEQ + 1)

    def xn_cur(k, a, b):
        return pbf[:, k * (SEQ + 1) + 1 + a: k * (SEQ + 1) + 1 + b]

    def xn_prv(k, a, b):
        return pbf[:, k * (SEQ + 1) + a: k * (SEQ + 1) + b]

    xnj = lambda p: pbf[:, P_XNJ: P_XNJ + D]
    kact3 = pbf[:, P_KACT:P_KACT + 4096].rearrange("q (k t) -> q k t", t=512)
    whalf3 = pbf[:, P_KACT:P_KACT + 4096].rearrange("q (k d) -> q k d", d=512)
    tanh4 = pbf[:, P_TANH:P_TANH + 2048].rearrange("q (a d) -> q a d", d=512)
    sqjunk = pbf[:, P_TANH:P_TANH + 1024]
    krelu = pbf[:, P_TANH:P_TANH + 512]
    xabA = pbf[:, P_XAB:P_XAB + 4096].rearrange("q (k t) -> q k t", t=512)
    xabB = pbf[:, P_XAB2:P_XAB2 + 4096].rearrange("q (k t) -> q k t", t=512)
    cblk = lambda p: pbf[:, P_XAB + p * 1024: P_XAB + (p + 1) * 1024].rearrange(
        "q (k d) -> q k d", d=128)
    Eb = pbf[:, 0:2048]
    eT = pbf[:, 2048:4096]
    hT_flat = hTok.rearrange("q a b -> q (a b)")
    wc = lambda p: hT_flat[:, p * D:(p + 1) * D]
    wt = lambda p: hT_flat[:, 4096 + p * D: 4096 + (p + 1) * D]
    fT_flat = fT.rearrange("q a b -> q (a b)")
    gT_flat = gT.rearrange("q a b -> q (a b)")
    wkf_w = fT_flat[:, 0:8192].rearrange("q (k d) -> q k d", d=D)
    wr_w = fT_flat[:, 8192:16384].rearrange("q (k d) -> q k d", d=D)
    wvf_w = gT_flat[:, 0:8192].rearrange("q (k d) -> q k d", d=D)

    x2b = pf[:, 0:2048].rearrange("q (a c) -> q a c", c=D)
    x2balt = pf[:, 2048:4096].rearrange("q (a c) -> q a c", c=D)
    x_in = lambda p: pf[:, p * D:(p + 1) * D]
    outst = lambda p: pf[:, 2048 + p * D: 2048 + (p + 1) * D]
    identf = pf[:, 2048:2176]
    mur8 = pf[0:8, 2176:2304]
    muk8 = pf[0:8, 2304:2432]
    ones1 = pf[0:1, 2432:2560]
    g1 = pf[0:1, 2560:2561]

    sem_names = ("d_init", "d_wc0", "d_wc1", "d_wc2", "d_wc3", "d_x2c",
                 "d_rpkf", "d_rpr", "d_rpvf",
                 "d_rp2", "d_rp3", "d_x2b", "d_cb0", "d_cb1", "d_wh", "d_fw0",
                 "d_fw1", "d_fw2", "d_xi0", "d_xi1", "d_st0", "d_st1",
                 "t", "v", "s", "g")
    SEM = {nm: stack.enter_context(nc.semaphore(nm)) for nm in sem_names}

    ops = {"sync": [], "gpsimd": [], "scalar": [], "vector": [], "tensor": []}
    cnt = {nm: 0 for nm in sem_names}

    def rec(eng, fn, inc=None, by=1):
        if inc is None:
            ops[eng].append(fn)
            return None
        cnt[inc] += by
        val = cnt[inc]
        sem = SEM[inc]

        def run(e, fn=fn, sem=sem, by=by):
            fn(e).then_inc(sem, by)
        ops[eng].append(run)
        return val

    def wait(eng, sem, val):
        if val is not None and val > 0:
            ops[eng].append(lambda e, s=SEM[sem], v=val: e.wait_ge(s, v))

    def fence_all():
        for e in ops:
            for snm in sem_names:
                wait(e, snm, cnt[snm])

    def dma(queue_eng, qsem, out_ap, in_ap):
        return rec(queue_eng, lambda e, o=out_ap, i=in_ap: e.dma_start(o, i),
                   inc=qsem, by=16)

    def vchain(fn):
        vv = rec("vector", fn, inc="v")
        wait("vector", "v", vv)
        return vv

    rfence = {}

    def rwait(eng, name):
        f = rfence.get(name)
        if f is not None:
            wait(eng, f[0], f[1])

    def rset(name, sem, val):
        rfence[name] = (sem, val)

    dma("sync", "d_init", identf, idf_h[:, :])
    dma("sync", "d_init", mur8, mur_h.ap().rearrange("(k p) -> k p", p=128))
    dma("sync", "d_init", muk8, muk_h.ap().rearrange("(k p) -> k p", p=128))
    dma("sync", "d_init", g1, gam_h[0:1])

    rec("gpsimd", lambda e: e.memset(ones1, 1.0), inc="g")

    wait("scalar", "d_init", 64)
    v_idb = rec("scalar", lambda e: e.copy(identb[:, :], identf), inc="s")

    wait("tensor", "d_init", 64)
    rec("tensor", lambda e: e.transpose(ps_am[:, 0:8], mur8, identf[0:8, 0:8]))
    rec("tensor", lambda e: e.transpose(ps_am[:, 8:16], muk8, identf[0:8, 0:8]))
    wait("tensor", "g", 1)
    t_mu = rec("tensor", lambda e: e.matmul(ps_am[:, 16:17], ones1, g1,
                                            start=True, stop=True), inc="t")
    wait("vector", "t", t_mu)
    vchain(lambda e: e.tensor_copy(mus[:, 0:8], ps_am[:, 0:8]))
    vchain(lambda e: e.tensor_copy(mus[:, 8:16], ps_am[:, 8:16]))
    vchain(lambda e: e.tensor_copy(gam128[:, 0:1], ps_am[:, 16:17]))
    vchain(lambda e: e.tensor_scalar(mus[:, 16:24], mus[:, 0:8], -1.0, 1.0, AL.mult, AL.add))
    v_mu = vchain(lambda e: e.tensor_scalar(mus[:, 24:32], mus[:, 8:16], -1.0, 1.0,
                                            AL.mult, AL.add))
    rset("ps_am", "v", v_mu)

    def emit_pass1_group(sq, grp):
        row0 = sq * SEQ
        p = grp % 2
        buf = x2b if p == 0 else x2balt
        dsem = "d_x2b" if p == 0 else "d_x2c"
        rwait("sync", "x2b_v%d" % p)
        rwait("sync", "x2b_s%d" % p)
        if p == 1:
            wait("sync", "t", t_mu)
            wait("sync", "s", v_idb)
        qv = dma("sync", dsem, buf,
                 x_h[row0 + grp * 256: row0 + (grp + 1) * 256, :]
                 .rearrange("(a p) c -> p a c", p=128))
        wait("vector", dsem, qv)
        rec("vector", lambda e, grp=grp, buf=buf: e.reduce_sum(
            sstat[:, 2 * grp:2 * grp + 1], buf[:, 0, :], axis=AX.X))
        v_red = rec("vector", lambda e, grp=grp, buf=buf: e.reduce_sum(
            sstat[:, 2 * grp + 1:2 * grp + 2], buf[:, 1, :], axis=AX.X), inc="v")
        wait("scalar", dsem, qv)
        rwait("scalar", "sqjunk")
        rec("scalar", lambda e, grp=grp, buf=buf: e.activation(
            sqjunk, buf[:, 0, :], AF.Square,
            accum_out=sstat[:, NT + 2 * grp:NT + 2 * grp + 1]))
        s_sq = rec("scalar", lambda e, grp=grp, buf=buf: e.activation(
            sqjunk, buf[:, 1, :], AF.Square,
            accum_out=sstat[:, NT + 2 * grp + 1:NT + 2 * grp + 2]), inc="s")
        rset("x2b_v%d" % p, "v", v_red)
        rset("x2b_s%d" % p, "s", s_sq)
        return v_red, s_sq

    def emit_stats(sq, v_red, s_sq):
        base = sq * NT
        wait("vector", "s", s_sq)
        wait("vector", "v", v_red)
        mean_ = lns[:, 0:16]
        ex2_ = lns[:, 16:32]
        msq_ = lns[:, 32:48]
        veps_ = lns[:, 48:64]
        t1_ = lns[:, 64:80]
        t2_ = lns[:, 80:96]
        y_ = rstd[:, :]
        vchain(lambda e: e.tensor_scalar_mul(lns[:, 0:32], sstat[:, :], 1.0 / D))
        vchain(lambda e: e.tensor_mul(msq_, mean_, mean_))
        vchain(lambda e: e.tensor_sub(veps_, ex2_, msq_))
        vchain(lambda e: e.tensor_scalar_add(veps_, veps_, EPS))
        vchain(lambda e: e.tensor_scalar(t1_, veps_, 0.5, 0.5, AL.mult, AL.add))
        vchain(lambda e: e.reciprocal(y_, t1_))
        for _ in range(NEWTON):
            vchain(lambda e: e.tensor_mul(t1_, y_, y_))
            vchain(lambda e: e.tensor_mul(t2_, t1_, veps_))
            vchain(lambda e: e.tensor_scalar(t1_, t2_, -0.5, 1.5, AL.mult, AL.add))
            vchain(lambda e: e.tensor_mul(y_, y_, t1_))
        vchain(lambda e: e.tensor_scalar_add(s1a[:, base:base + NT], y_, 1.0))
        return vchain(lambda e: e.tensor_mul(aa[:, base:base + NT], mean_, y_))

    v_xnT = {}

    def emit_pass2_group(sq, grp):
        row0 = sq * SEQ
        p = grp % 2
        buf = x2b if p == 0 else x2balt
        dsem = "d_x2b" if p == 0 else "d_x2c"
        rwait("sync", "x2b_v%d" % p)
        qv = dma("sync", dsem, buf,
                 x_h[row0 + grp * 256: row0 + (grp + 1) * 256, :]
                 .rearrange("(a p) c -> p a c", p=128))
        wait("vector", dsem, qv)
        base = sq * NT
        v_last = None
        for i in (0, 1):
            ti = 2 * grp + i
            rwait("vector", "xnj0")
            v_norm = rec("vector", lambda e, i=i, ti=ti, base=base, buf=buf: e.tensor_scalar(
                xnj(0), buf[:, i, :],
                rstd[:, ti:ti + 1], aa[:, base + ti:base + ti + 1],
                AL.mult, AL.subtract), inc="v")
            wait("tensor", "v", v_norm)
            wait("tensor", "s", v_idb)
            rwait("tensor", "ps_tr%d" % (ti % 2))
            t_xnt = None
            for kk in range(KC):
                t_xnt = rec("tensor", lambda e, ti=ti, kk=kk: e.transpose(
                    ps_tr[ti % 2][:, kk * 128:(kk + 1) * 128],
                    xnj(0)[:, kk * 128:(kk + 1) * 128], identb[:, :]),
                    inc="t" if kk == KC - 1 else None)
            rset("xnj0", "t", t_xnt)
            wait("vector", "t", t_xnt)
            v_xnT[ti] = rec("vector", lambda e, ti=ti: e.tensor_copy(
                xnT3[:, :, 1 + ti * 128: 1 + (ti + 1) * 128],
                ps_tr[ti % 2][:, :].rearrange("q (k t) -> q k t", t=128)), inc="v")
            rset("ps_tr%d" % (ti % 2), "v", v_xnT[ti])
            v_last = v_xnT[ti]
        rset("x2b_v%d" % p, "v", v_last)

    def emit_shift_build(tt, mu_off, xab):
        for k in range(KC):
            rec("vector", lambda e, k=k, tt=tt, xab=xab: e.tensor_sub(
                xab[:, k, :],
                xn_cur(k, tt * 512, (tt + 1) * 512),
                xn_prv(k, tt * 512, (tt + 1) * 512)))
        v_x = None
        for k in range(KC):
            v_x = rec("vector", lambda e, k=k, tt=tt, mu_off=mu_off, xab=xab:
                      e.scalar_tensor_tensor(
                xab[:, k, :], xab[:, k, :], mus[:, mu_off + k:mu_off + k + 1],
                xn_prv(k, tt * 512, (tt + 1) * 512), AL.mult, AL.add), inc="v")
        return v_x

    last_v_ka = [None]

    def emit_K(tt, v_xk, qfw):
        t_K_last = None
        for j in range(KC):
            b = j % 4
            wait("tensor", "d_fw0", qfw[0])
            wait("tensor", "v", v_xk)
            rwait("tensor", "bank%d" % b)
            t_last = None
            for k in range(KC):
                t_last = rec("tensor", lambda e, j=j, k=k, b=b: e.matmul(
                    ps_sc[:, b * 512:(b + 1) * 512],
                    wkf_w[:, k, j * 128:(j + 1) * 128], xabA[:, k, :],
                    start=(k == 0), stop=(k == KC - 1)),
                    inc="t" if k == KC - 1 else None)
            t_K_last = t_last
            wait("scalar", "t", t_last)
            if last_v_ka[0] is not None:
                wait("scalar", "v", last_v_ka[0])
            s_kr = rec("scalar", lambda e, b=b: e.activation(
                krelu, ps_sc[:, b * 512:(b + 1) * 512], AF.Relu), inc="s")
            rset("bank%d" % b, "s", s_kr)
            wait("vector", "s", s_kr)
            last_v_ka[0] = rec("vector", lambda e, j=j: e.tensor_mul(
                kact3[:, j, :], krelu, krelu), inc="v")
        rset("xabA", "t", t_K_last)
        return t_K_last

    def emit_R(tt, h, v_xr, qfw):
        t_R_last = None
        s_tanh = {}
        for tq in range(4):
            ab = tq % 2
            wait("tensor", "d_fw1", qfw[1])
            wait("tensor", "v", v_xr)
            rwait("tensor", "amb%d" % ab)
            t_last = None
            for k in range(KC):
                t_last = rec("tensor", lambda e, k=k, tq=tq, ab=ab, h=h: e.matmul(
                    ps_am[:, ab * 512:(ab + 1) * 512],
                    xabB[:, k, tq * 128:(tq + 1) * 128],
                    wr_w[:, k, h * 512:(h + 1) * 512],
                    start=(k == 0), stop=(k == KC - 1)),
                    inc="t" if k == KC - 1 else None)
            t_R_last = t_last
            wait("scalar", "t", t_last)
            rwait("scalar", "tanh%d" % tq)
            s_tanh[tq] = rec("scalar", lambda e, tq=tq, ab=ab: e.activation(
                tanh4[:, tq, :], ps_am[:, ab * 512:(ab + 1) * 512],
                AF.Tanh, scale=0.5), inc="s")
            rset("amb%d" % ab, "s", s_tanh[tq])
        return t_R_last, s_tanh

    def emit_V2(tt, h, s_tanh, qfw):
        t_V_last = None
        for tq in range(4):
            b = tq % 4
            wait("tensor", "d_fw2", qfw[2])
            rwait("tensor", "bank%d" % b)
            t_last = None
            for k in range(KC):
                t_last = rec("tensor", lambda e, k=k, tq=tq, b=b, h=h: e.matmul(
                    ps_sc[:, b * 512:(b + 1) * 512],
                    kact3[:, k, tq * 128:(tq + 1) * 128],
                    wvf_w[:, k, h * 512:(h + 1) * 512],
                    start=(k == 0), stop=(k == KC - 1)),
                    inc="t" if k == KC - 1 else None)
            t_V_last = t_last
            wait("vector", "t", t_last)
            wait("vector", "s", s_tanh[tq])
            vv = rec("vector", lambda e, tt=tt, tq=tq, h=h, b=b: e.scalar_tensor_tensor(
                ffnO[:, tt * 4 + tq, h * 512:(h + 1) * 512],
                tanh4[:, tq, :], 1.0, ps_sc[:, b * 512:(b + 1) * 512],
                AL.add, AL.mult), inc="v")
            rset("bank%d" % b, "v", vv)
            rset("tanh%d" % tq, "v", vv)
            rset("sqjunk", "v", vv)
        return t_V_last

    repack_list = [("wkf", False), ("wr", False), ("wvf", True),
                   ("wq", False), ("wk", False), ("wv", False)]
    rp_iter = [(nm, sc, j) for nm, sc in repack_list for j in range(KC)]

    def emit_repack(i):
        nm, scale_half, j = rp_iter[i]
        par = i % 4
        rp_sem = (("d_rpkf", "d_rpr", "d_rpvf")[i // 8] if i < 24
                  else "d_rp%d" % ((i % 2) + 2))
        rwait("gpsimd", "wcast%d" % par)
        qv = dma("gpsimd", "d_wc%d" % par, wc(par),
                 w_h[nm][j * 128:(j + 1) * 128, :])
        wait("tensor", "d_wc%d" % par, qv)
        wait("tensor", "s", v_idb)
        rwait("tensor", "ps_tr%d" % (par % 2))
        tv = None
        for kk in range(KC):
            tv = rec("tensor",
                     lambda e, par=par, kk=kk: e.transpose(
                         ps_tr[par % 2][:, kk * 128:(kk + 1) * 128],
                         wc(par)[:, kk * 128:(kk + 1) * 128], identb[:, :]),
                     inc="t" if kk == KC - 1 else None)
        rset("wcast%d" % par, "t", tv)
        wait("scalar", "t", tv)
        rwait("scalar", "wtmp%d" % par)
        if scale_half:
            sv2 = rec("scalar", lambda e, par=par: e.mul(
                wt(par), ps_tr[par % 2][:, :], 0.5), inc="s")
        else:
            sv2 = rec("scalar", lambda e, par=par: e.copy(
                wt(par), ps_tr[par % 2][:, :]), inc="s")
        rset("ps_tr%d" % (par % 2), "s", sv2)
        wait("sync", "s", sv2)
        sv = dma("sync", rp_sem,
                 wT_h[nm][:, j * 128:(j + 1) * 128].rearrange("(k p) d -> p k d", p=128),
                 wt(par).rearrange("q (k d) -> q k d", d=128))
        rset("wtmp%d" % par, rp_sem, cnt[rp_sem])

    v_red = s_sq = None
    for grp in range(8):
        v_red, s_sq = emit_pass1_group(0, grp)
    for i in range(24):
        emit_repack(i)
    emit_stats(0, v_red, s_sq)

    for sq in range(NSEQ):
        row0 = sq * SEQ
        if sq > 0:
            fence_all()
            rfence.clear()

        for k in range(KC):
            rec("gpsimd", lambda e, k=k: e.memset(xnT3[:, k, 0:1], 0.0),
                inc="g" if k == KC - 1 else None)
        g_z = cnt["g"]

        last_v_ka[0] = None
        emit_pass2_group(sq, 0)
        emit_pass2_group(sq, 1)
        rwait("sync", "fTw")
        rwait("sync", "gTw")
        wait("sync", "d_rpkf", 16 * 8)
        q0 = dma("sync", "d_fw0", wkf_w,
                 wT_h["wkf"].ap().rearrange("(k p) d -> p k d", p=128))
        wait("sync", "d_rpr", 16 * 8)
        q1 = dma("sync", "d_fw1", wr_w,
                 wT_h["wr"].ap().rearrange("(k p) d -> p k d", p=128))
        wait("sync", "d_rpvf", 16 * 8)
        q2 = dma("sync", "d_fw2", wvf_w,
                 wT_h["wvf"].ap().rearrange("(k p) d -> p k d", p=128))
        qfw = (q0, q1, q2)
        wait("vector", "g", g_z)
        rwait("vector", "xabA")
        wait("vector", "v", v_xnT[3])
        v_xk = emit_shift_build(0, 8, xabA)
        rwait("vector", "xabB")
        v_xr = emit_shift_build(0, 0, xabB)
        t_K_f = t_R_f = t_V_f = None
        for tt in range(FT):
            t_K_f = emit_K(tt, v_xk, qfw)
            if sq == 0:
                for i in range(24 + 6 * tt, 24 + 6 * tt + 6):
                    emit_repack(i)
            if 2 * tt + 2 < 8:
                emit_pass2_group(sq, 2 * tt + 2)
                emit_pass2_group(sq, 2 * tt + 3)
            t_R0, s_tanh0 = emit_R(tt, 0, v_xr, qfw)
            if tt + 1 < FT:
                rwait("vector", "xabA")
                wait("vector", "g", g_z)
                wait("vector", "v", v_xnT[4 * tt + 7])
                v_xk = emit_shift_build(tt + 1, 8, xabA)
            t_V0 = emit_V2(tt, 0, s_tanh0, qfw)
            t_R_f, s_tanh1 = emit_R(tt, 1, v_xr, qfw)
            rset("xabB", "t", t_R_f)
            if tt + 1 < FT:
                rwait("vector", "xabB")
                v_xr = emit_shift_build(tt + 1, 0, xabB)
            t_V_f = emit_V2(tt, 1, s_tanh1, qfw)
            rset("kact", "t", t_V_f)
        if stop == "pa3" and sq == 0:
            break

        fence_fT = ("t", t_R_f)
        fence_gT = ("t", t_V_f)
        mgrp = [0]
        p1_grp = [0]
        v_red1 = [None]
        s_sq1 = [None]
        for nm, dest, wfence in (("wq", fT, fence_fT), ("wk", gT, fence_gT)):
            for j in range(KC):
                par = j % 2
                if sq == 0 and j % 2 == 0 and p1_grp[0] < 8:
                    v_red1[0], s_sq1[0] = emit_pass1_group(1, p1_grp[0])
                    p1_grp[0] += 1
                rwait("sync", "colblk%d" % par)
                if j < 2:
                    wait("sync", "t", t_K_f)
                    wait("sync", "d_rp2", 16 * 12)
                    wait("sync", "d_rp3", 16 * 12)
                qv = dma("sync", "d_cb%d" % par, cblk(par),
                         wT_h[nm][:, j * 128:(j + 1) * 128].rearrange("(k p) d -> p k d", p=128))
                t_last = None
                for tt in range(4):
                    b = mgrp[0] % 4
                    mgrp[0] += 1
                    wait("tensor", "d_cb%d" % par, qv)
                    wait("tensor", "v", v_xnT[4 * tt + 3])
                    rwait("tensor", "bank%d" % b)
                    for k in range(KC):
                        t_last = rec("tensor", lambda e, par=par, k=k, tt=tt, b=b: e.matmul(
                            ps_sc[:, b * 512:(b + 1) * 512],
                            cblk(par)[:, k, :],
                            xn_cur(k, tt * 512, (tt + 1) * 512),
                            start=(k == 0), stop=(k == KC - 1)),
                            inc="t" if k == KC - 1 else None)
                    wait("vector", "t", t_last)
                    if wfence is not None:
                        wait("vector", *wfence)
                        wfence = None
                    vv = rec("vector", lambda e, dest=dest, j=j, tt=tt, b=b: e.tensor_copy(
                        dest[:, j, tt * 512:(tt + 1) * 512],
                        ps_sc[:, b * 512:(b + 1) * 512]), inc="v")
                    rset("bank%d" % b, "v", vv)
                rset("colblk%d" % par, "t", t_last)

        for h in range(2):
            rwait("sync", "whalf")
            wait("sync", "t", t_V_f)
            qv = dma("sync", "d_wh", whalf3,
                     wT_h["wv"][:, h * 512:(h + 1) * 512].rearrange("(k p) d -> p k d", p=128))
            t_last = None
            for tq in range(NT):
                b = mgrp[0] % 4
                mgrp[0] += 1
                wait("tensor", "d_wh", qv)
                wait("tensor", "v", v_xnT[tq])
                rwait("tensor", "bank%d" % b)
                for k in range(KC):
                    t_last = rec("tensor", lambda e, k=k, tq=tq, b=b: e.matmul(
                        ps_sc[:, b * 512:(b + 1) * 512],
                        xn_cur(k, tq * 128, (tq + 1) * 128),
                        whalf3[:, k, :],
                        start=(k == 0), stop=(k == KC - 1)),
                        inc="t" if k == KC - 1 else None)
                wait("vector", "t", t_last)
                vv = rec("vector", lambda e, tq=tq, h=h, b=b: e.tensor_copy(
                    hTok[:, tq, h * 512:(h + 1) * 512],
                    ps_sc[:, b * 512:(b + 1) * 512]), inc="v")
                rset("bank%d" % b, "v", vv)
            rset("whalf", "t", t_last)
        if sq == 0:
            emit_stats(1, v_red1[0], s_sq1[0])

        fence_all()
        rfence.clear()
        qx = {0: dma("sync", "d_xi0", x_in(0), x_h[row0:row0 + 128, :]),
              1: dma("sync", "d_xi1", x_in(1), x_h[row0 + 128:row0 + 256, :])}
        t_S, t_ET, t_AM = {}, {}, {}
        v_mx, v_ev, v_cmb = {}, {}, {}
        s_exp, st_val = {}, {}

        def emit_AM(qq):
            wait("tensor", "v", v_ev[(qq, 1)])
            if qq >= 1:
                wait("tensor", "v", v_cmb[qq - 1])
            tv = None
            for h in range(2):
                for sc in range(NT):
                    tv = rec("tensor", lambda e, sc=sc, h=h: e.matmul(
                        ps_am[:, h * 512:(h + 1) * 512],
                        eT[:, sc * 128:(sc + 1) * 128],
                        hTok[:, sc, h * 512:(h + 1) * 512],
                        start=(sc == 0), stop=(sc == NT - 1)),
                        inc="t" if (h == 1 and sc == NT - 1) else None)
            t_AM[qq] = tv

        def emit_combine(qq):
            wait("vector", "t", t_AM[qq])
            wait("vector", "d_xi%d" % (qq % 2), qx[qq])
            wait("vector", "s", s_exp[qq])
            if qq >= 2:
                wait("vector", "d_st%d" % (qq % 2), st_val[qq - 2])
            vchain(lambda e, qq=qq: e.reciprocal(sinv[:, 0:1],
                                                 sumexp[:, qq % 2:qq % 2 + 1]))
            vchain(lambda e: e.tensor_scalar_mul(factor[:, 0:1], sinv[:, 0:1],
                                                 gam128[:, 0:1]))
            base = sq * NT
            rec("vector", lambda e, qq=qq, base=base: e.tensor_scalar(
                outst(qq % 2), x_in(qq % 2),
                s1a[:, base + qq:base + qq + 1], aa[:, base + qq:base + qq + 1],
                AL.mult, AL.subtract))
            rec("vector", lambda e, qq=qq: e.scalar_tensor_tensor(
                outst(qq % 2), ps_am[:, 0:1024], factor[:, 0:1],
                outst(qq % 2), AL.mult, AL.add))
            v_cmb[qq] = rec("vector", lambda e, qq=qq: e.tensor_add(
                outst(qq % 2), outst(qq % 2), ffnO[:, qq, :]), inc="v")

        def emit_store(qq):
            wait("scalar", "v", v_cmb[qq])
            st_val[qq] = dma("scalar", "d_st%d" % (qq % 2),
                             out_h[row0 + qq * 128: row0 + (qq + 1) * 128, :],
                             outst(qq % 2))

        for q in range(NT):
            if q >= 1:
                wait("tensor", "s", s_exp[q - 1])
            tv = None
            for j in range(4):
                for k in range(KC):
                    tv = rec("tensor", lambda e, j=j, k=k, q=q: e.matmul(
                        ps_sc[:, j * 512:(j + 1) * 512],
                        gT[:, k, q * 128:(q + 1) * 128],
                        fT[:, k, j * 512:(j + 1) * 512],
                        start=(k == 0), stop=(k == KC - 1)),
                        inc="t" if (j == 3 and k == KC - 1) else None)
            t_S[q] = tv
            wait("vector", "t", t_S[q])
            v_mx[q] = rec("vector", lambda e, q=q: e.reduce_max(
                negmax[:, q % 2:q % 2 + 1], ps_sc[:, 0:2048], axis=AX.X, negate=True),
                inc="v")
            wait("scalar", "v", v_mx[q])
            s_exp[q] = rec("scalar", lambda e, q=q: e.activation(
                Eb, ps_sc[:, 0:2048], AF.Exp,
                bias=negmax[:, q % 2:q % 2 + 1], scale=1.0,
                accum_out=sumexp[:, q % 2:q % 2 + 1]), inc="s")
            if q >= 1:
                emit_AM(q - 1)
            wait("tensor", "s", s_exp[q])
            for b in range(2):
                rwait("tensor", "ps_tr%d" % b)
                tv = None
                for kk in range(KC):
                    tv = rec("tensor", lambda e, b=b, kk=kk: e.transpose(
                        ps_tr[b][:, kk * 128:(kk + 1) * 128],
                        Eb[:, (b * KC + kk) * 128:(b * KC + kk + 1) * 128],
                        identb[:, :]), inc="t" if kk == KC - 1 else None)
                t_ET[(q, b)] = tv
            if q >= 1:
                emit_combine(q - 1)
            for b in range(2):
                wait("vector", "t", t_ET[(q, b)])
                v_ev[(q, b)] = rec("vector", lambda e, b=b: e.tensor_copy(
                    eT[:, b * 1024:(b + 1) * 1024], ps_tr[b][:, :]), inc="v")
                rset("ps_tr%d" % b, "v", v_ev[(q, b)])
            if q >= 1:
                emit_store(q - 1)
            if q + 1 < NT and q >= 1:
                wait("sync", "v", v_cmb[q - 1])
                qx[q + 1] = dma("sync", "d_xi%d" % ((q + 1) % 2), x_in((q + 1) % 2),
                                x_h[row0 + (q + 1) * 128: row0 + (q + 2) * 128, :])
        emit_AM(NT - 1)
        emit_combine(NT - 1)
        emit_store(NT - 1)
        rset("fTw", "t", t_S[NT - 1])
        rset("gTw", "t", t_S[NT - 1])
        if stop == "seq0" and sq == 0:
            break

    if debug:
        dbg_h = nc.dram_tensor("dbg", [128, 4096], F32, kind="ExternalOutput")
        dbgb_h = nc.dram_tensor("dbgb", [128, 16384], BF16, kind="ExternalOutput")
        fence_all()

        def dmad(o, i):
            return rec("sync", lambda e, o=o, i=i: e.dma_start(
                o, i, allow_slow_non_contiguous=True), inc="d_init", by=16)
        dmad(dbg_h[:, 0:2], negmax[:, :])
        dmad(dbg_h[:, 2:4], sumexp[:, :])
        dmad(dbg_h[:, 4:5], sinv[:, :])
        dmad(dbg_h[:, 5:6], factor[:, :])
        dmad(dbg_h[:, 6:7], gam128[:, :])
        dmad(dbg_h[:, 16:48], mus[:, :])
        dmad(dbg_h[:, 48:64], s1a[:, :])
        dmad(dbg_h[:, 64:80], aa[:, :])
        dmad(dbg_h[:, 80:176], lns[:, :])
        dmad(dbg_h[:, 176:192], rstd[:, :])
        dmad(dbg_h[:, 192:208], sums[:, :])
        dmad(dbg_h[:, 208:224], ssq[:, :])
        dmad(dbg_h[:, 1152:2176], outst(1))
        dmad(dbgb_h[:, 0:2048], Eb)
        dmad(dbgb_h[:, 2048:4096], eT)
        dmad(dbgb_h[:, 4096:5120], pbf[:, 0:1024])
        dmad(dbgb_h[:, 5120:6144], fT[:, 0, 0:1024])
        dmad(dbgb_h[:, 6144:7168], gT[:, 0, 0:1024])
        dmad(dbgb_h[:, 7168:7296], identb[:, :])
        dmad(dbgb_h[:, 8192:9216], hTok[:, 0, :])
        dmad(dbgb_h[:, 9216:10240], ffnO[:, 0, :])
        dmad(dbgb_h[:, 10240:14336], kact3[:, :, :])
        dmad(dbgb_h[:, 14336:15360], xnj(0))
        wait("sync", "d_init", cnt["d_init"])

    wait("scalar", "d_st0", cnt["d_st0"])
    wait("scalar", "d_st1", cnt["d_st1"])

    rem = nc.sbuf_bytes_remaining
    assert rem >= 0, f"SBUF over budget by {-rem} bytes"

    with nc.Block() as block:
        @block.sync
        def _(e):
            for f in ops["sync"]:
                f(e)

        @block.gpsimd
        def _(e):
            for f in ops["gpsimd"]:
                f(e)

        @block.scalar
        def _(e):
            for f in ops["scalar"]:
                f(e)

        @block.vector
        def _(e):
            for f in ops["vector"]:
                f(e)

        @block.tensor
        def _(e):
            for f in ops["tensor"]:
                f(e)

    return nc


def kernel(L0_attn_x, L0_attn_Wq, L0_attn_Wk, L0_attn_Wv, L0_attn_gamma,
           L0_ffn_W_r, L0_ffn_W_k, L0_ffn_W_v, L0_ffn_mu_r, L0_ffn_mu_k):
    x = np.ascontiguousarray(np.asarray(L0_attn_x, dtype=np.float32))
    ws = {
        "wq": np.ascontiguousarray(np.asarray(L0_attn_Wq, dtype=np.float32)),
        "wk": np.ascontiguousarray(np.asarray(L0_attn_Wk, dtype=np.float32)),
        "wv": np.ascontiguousarray(np.asarray(L0_attn_Wv, dtype=np.float32)),
        "wr": np.ascontiguousarray(np.asarray(L0_ffn_W_r, dtype=np.float32)),
        "wkf": np.ascontiguousarray(np.asarray(L0_ffn_W_k, dtype=np.float32)),
        "wvf": np.ascontiguousarray(np.asarray(L0_ffn_W_v, dtype=np.float32)),
    }
    mur = np.ascontiguousarray(np.asarray(L0_ffn_mu_r, dtype=np.float32))
    muk = np.ascontiguousarray(np.asarray(L0_ffn_mu_k, dtype=np.float32))
    gam = np.ascontiguousarray(np.asarray(L0_attn_gamma, dtype=np.float32))
    ident = np.eye(128, dtype=np.float32)

    nc = build_nc()
    in_maps = []
    for c in range(8):
        m = {"x": x[c * NSEQ:(c + 1) * NSEQ].reshape(NSEQ * SEQ, D),
             "mur": mur, "muk": muk, "gamma": gam, "ident": ident}
        m.update(ws)
        in_maps.append(m)
    res = run_bass_kernel_spmd(nc, in_maps, core_ids=list(range(8)))
    out = np.stack([np.asarray(r["out"]) for r in res.results])
    return out.reshape(16, SEQ, D)
